# version 06
# baseline (speedup 1.0000x reference)
import os
import numpy as np
import concourse.bass as bass
import concourse.mybir as mybir
from concourse.bass_utils import run_bass_kernel_spmd

F32 = mybir.dt.float32
BF16 = mybir.dt.bfloat16
I32 = mybir.dt.int32
U8 = mybir.dt.uint8
AF = mybir.ActivationFunctionType
ALU = mybir.AluOpType
AX = mybir.AxisListType

D = 2048; H = 2048; FI = 6144; FF = 8192
T = 1024; HALO = 4; TE = T + HALO
ND, NH, NF, NFF = 16, 16, 48, 64
QT = 256; NQ = 4
EPS_LN = 1e-5; EPS_RMS = 1e-8; EPS_Q = 1e-5; C_RGLRU = 8.0

_off = [0]
def _carve(n, a=64):
    o = _off[0]; _off[0] = (o + n + a - 1) // a * a
    return o

O_X    = _carve(16 * T * 4)
O_BA   = _carve(65536)
O_BB   = _carve(16 * TE * 2)
O_BC   = _carve(16 * T * 2)
O_WS   = _carve(8192)
O_STAT = _carve(TE * 4)
O_VEC  = _carve(192 * 4)
O_UPB  = _carve(96 * 4)
O_CW   = _carve(64 * 4)
O_EYE  = _carve(128 * 4)
O_ROW  = _carve(512)
O_HEND = _carve(64)
O_XH   = _carve(16 * HALO * 4)
O_MISC = _carve(640)
TOTAL_SB = _off[0]
assert TOTAL_SB <= 212858, TOTAL_SB

VC_LB, VC_RB, VC_CB, VC_GAB, VC_GXB, VC_OB, VC_DB, VC_SC = 0, 16, 32, 48, 64, 80, 96, 112
VC_N1W, VC_N2W = 128, 144
VC_ONESK, VC_MASK, VC_W2SCL, VC_WS1, VC_WS2, VC_W1SCL = 160, 161, 162, 163, 181, 182
VC_SEL, VC_RSEL = 164, 168
C_M1, C_P1, C_SC1, C_B13, C_ELN, C_ERMS, C_2, C_HALF, C_INVD = \
    172, 173, 174, 175, 176, 177, 178, 179, 180

SEMKEYS = ("sync", "gdma", "gcmp", "gcol", "tensor", "vector", "scalar")
ENG2SEM = {"sync": "sync", "tensor": "tensor", "vector": "vector", "scalar": "scalar"}


class Em:
    def __init__(self):
        self.prog = {e: [] for e in ("sync", "gpsimd", "tensor", "vector", "scalar")}
        self.cnt = {k: 0 for k in SEMKEYS}
        self.ew = {e: {k: 0 for k in SEMKEYS} for e in self.prog}
        self.flush_fn = {}
        self.flush_cache = {}

    def _flush(self, tok):
        key = tok
        ft = self.flush_cache.get(key)
        if ft is None:
            sk, v = tok
            eng = "scalar" if sk == "scalar" else "gpsimd"
            ft = self.op(eng, self.flush_fn[sk], after=[tok], kind="cmp",
                         tag=f"fl_{sk}{v}")
            self.flush_cache[key] = ft
        return ft

    def op(self, eng, fn, after=(), kind="cmp", tag=""):
        if self.flush_fn and not tag.startswith("fl_"):
            fixed = []
            for tok in after:
                if tok is not None:
                    sk, v = tok
                    if sk == "scalar" and eng != "scalar":
                        tok = self._flush(tok)
                    elif sk == "gcmp" and eng != "gpsimd":
                        tok = self._flush(tok)
                fixed.append(tok)
            after = fixed
        if eng == "sync":
            semkey, inc = "sync", 16
        elif eng == "gpsimd":
            semkey = {"dma": "gdma", "cmp": "gcmp", "col": "gcol"}[kind]
            inc = 16 if kind == "dma" else 1
        else:
            semkey, inc = ENG2SEM[eng], 1
        waits = {}
        for tok in after:
            if tok is None:
                continue
            sk, v = tok
            waits[sk] = max(waits.get(sk, 0), v)
        fw = []
        ew = self.ew[eng]
        for sk, v in sorted(waits.items()):
            if v > ew[sk]:
                fw.append((sk, v)); ew[sk] = v
        self.cnt[semkey] += inc
        self.prog[eng].append((fw, fn, semkey, inc, tag))
        return (semkey, self.cnt[semkey])

    def check(self):
        for eng, ops in self.prog.items():
            for fw, fn, sk_, inc, tag in ops:
                for sk, v in fw:
                    assert v <= self.cnt[sk], ("unreachable wait", eng, tag, sk, v)
        pos = {e: 0 for e in self.prog}
        cur = {k: 0 for k in SEMKEYS}
        prog = True
        while prog:
            prog = False
            for e, ops in self.prog.items():
                while pos[e] < len(ops):
                    fw, fn, semkey, inc, tag = ops[pos[e]]
                    if all(cur[sk] >= v for sk, v in fw):
                        cur[semkey] += inc; pos[e] += 1; prog = True
                    else:
                        break
        stuck = {e: p for e, p in pos.items() if p < len(self.prog[e])}
        if stuck:
            msg = []
            for e, p in stuck.items():
                fw, fn, semkey, inc, tag = self.prog[e][p]
                msg.append((e, p, tag, fw))
            raise RuntimeError(f"DEADLOCK: {msg} cur={cur}")

    def emit(self, block, sems):
        def mk(items):
            def body(e):
                for fw, fn, semkey, inc, tag in items:
                    for sk, v in fw:
                        e.wait_ge(sems[sk], v)
                    fn(e).then_inc(sems[semkey], inc)
            return body
        for eng, items in self.prog.items():
            if items:
                getattr(block, eng)(mk(items))


class Banks:
    def __init__(self, ids, free=None):
        self.ids = ids
        self.free = dict(free) if free else {b: None for b in ids}
        self.i = 0
    def take(self):
        b = self.ids[self.i % len(self.ids)]; self.i += 1
        return b, self.free.get(b)
    def done(self, b, tok):
        self.free[b] = tok


class Ring:
    def __init__(self, depth):
        self.depth = depth; self.free = [[] for _ in range(depth)]; self.i = 0
    def slot(self):
        s = self.i % self.depth; self.i += 1
        return s, list(self.free[s])
    def consumed(self, s, toks):
        self.free[s] = [t for t in toks if t is not None]


def build_nc():
    nc = bass.Bass("TRN2", target_bir_lowering=False)

    def din(name, shape, dtype=F32):
        return nc.dram_tensor(name, list(shape), dtype, kind="ExternalInput").ap()

    xin = din("xin", [D, TE])
    lw = din("lw", [D, H]); rw = din("rw", [D, H])
    gaw = din("gaw", [H, H]); gxw = din("gxw", [H, H]); ow = din("ow", [H, D])
    u1w = din("u1w", [D, FI]); u2w = din("u2w", [D, FI]); dwt = din("dwt", [FI, D])
    w1t = din("w1t", [D, FF]); w2t = din("w2t", [FF, D])
    w1sl = din("w1sl", [D, FF // 8]); w2sl = din("w2sl", [FF, D // 8])
    vecs = din("vecs", [128, 192]); upbv = din("upbv", [128, 96])
    cwv = din("cwv", [128, 64]); eyev = din("eyev", [128, 128])
    out_d = nc.dram_tensor("out", [D, T], F32, kind="ExternalOutput").ap()
    KDBG = bool(os.environ.get("KDBG"))
    dbg = {}
    if KDBG:
        for nm, shape, dt in (("dbg_xn", [128, TE], BF16), ("dbg_left", [128, T], BF16),
                              ("dbg_conv", [128, T], BF16), ("dbg_h", [128, T], F32),
                              ("dbg_x1", [128, T], F32), ("dbg_x1n", [128, T], BF16),
                              ("dbg_x2", [128, T], F32), ("dbg_qz", [128, T], BF16),
                              ("dbg_hq", [128, QT], BF16), ("dbg_stat", [1, TE], F32),
                              ("dbg_g", [1, T], F32), ("dbg_loga", [128, T], F32),
                              ("dbg_mult", [128, T], F32), ("dbg_i", [128, T], F32),
                              ("dbg_b", [128, T], F32), ("dbg_a", [128, T], F32),
                              ("dbg_gaps", [128, 512], F32), ("dbg_gaw", [128, 2048], BF16),
                              ("dbg_th", [128, T], F32), ("dbg_spr", [128, 80], F32),
                              ("dbg_glu", [128, T], BF16), ("dbg_dacc", [128, T], F32),
                              ("dbg_crow", [1, T], F32), ("dbg_mxrow", [1, T], F32),
                              ("dbg_s2r", [1, 512], F32), ("dbg_var", [1, T], F32),
                              ("dbg_s1raw", [1, 512], F32), ("dbg_m3", [1, 512], F32),
                              ("dbg_xpre", [128, 512], F32), ("dbg_u1ps", [128, 512], F32),
                              ("dbg_w1slot", [128, 2048], BF16), ("dbg_x1n4", [128, T], BF16),
                              ("dbg_rstd3", [1, T], F32), ("dbg_absmax", [1, T], F32)):
            dbg[nm] = nc.dram_tensor(nm, shape, dt, kind="ExternalOutput").ap()
    w1q = nc.dram_tensor("w1q", [NFF, 128, D], BF16, kind="Internal").ap()
    w2q = nc.dram_tensor("w2q", [NFF, 128, D], BF16, kind="Internal").ap()
    cc_in = nc.dram_tensor("cc_in", [128, 64], F32, kind="Internal").ap()
    cc_out = nc.dram_tensor("cc_out", [128, 64], F32, kind="Internal").ap()
    ar_in = nc.dram_tensor("ar_in", [1, 2], F32, kind="Internal").ap()
    ar_out = nc.dram_tensor("ar_out", [1, 2], F32, kind="Internal").ap()

    from contextlib import ExitStack
    with ExitStack() as stk:
        E = stk.enter_context
        arena = E(nc.sbuf_tensor("arena", [128, TOTAL_SB], U8)).ap()
        pb = [E(nc.psum_tensor(f"pb{i}", [128, 512], F32)) for i in range(8)]
        PS = [b.ap() for b in pb]
        sems = {k: E(nc.semaphore(k)) for k in SEMKEYS}
        block = E(nc.Block())
        em = Em()
        em.flush_fn = {
            "scalar": lambda e: e.activation(
                arena[32:33, O_MISC + 568:O_MISC + 572].bitcast(F32),
                arena[32:33, O_MISC + 568:O_MISC + 572].bitcast(F32), AF.Copy),
            "gcmp": lambda e: e.tensor_copy(
                arena[64:65, O_MISC + 568:O_MISC + 572].bitcast(F32),
                arena[64:65, O_MISC + 568:O_MISC + 572].bitcast(F32)),
        }

        Xf = arena[:, O_X:O_X + 65536].bitcast(F32)
        def Xc(j, lo=0, hi=T):
            return Xf[:, j * T + lo:j * T + hi]
        BAf = arena[:, O_BA:O_BA + 65536].bitcast(F32)
        BAbf = arena[:, O_BA:O_BA + 65536].bitcast(BF16)
        def bav(off, nbytes, dtype):
            return arena[:, O_BA + off:O_BA + off + nbytes].bitcast(dtype)
        def X1c(j, lo=0, hi=T):
            return BAf[:, j * T + lo:j * T + hi]
        SCR = [bav(i * 4096, 4096, F32) for i in range(8)]
        RP = bav(24576, 4112, F32)
        SROW1 = RP[0:1, 0:TE]
        CONVB = bav(32768, 32768, BF16)
        def CVc(j, lo=0, hi=T):
            return CONVB[:, j * T + lo:j * T + hi]
        BB = arena[:, O_BB:O_BB + 32896].bitcast(BF16)
        def XNc(j, lo=0, hi=TE):
            return BB[:, j * TE + lo:j * TE + hi]
        def Qc(j, lo=0, hi=T):
            return BB[:, j * T + lo:j * T + hi]
        X1Nc = Qc
        QZc = Qc
        BCbf = arena[:, O_BC:O_BC + 32768].bitcast(BF16)
        BCf = arena[:, O_BC:O_BC + 32768].bitcast(F32)
        BCi = arena[:, O_BC:O_BC + 32768].bitcast(I32)
        def Lc(j, lo=0, hi=T):
            return BCbf[:, j * T + lo:j * T + hi]
        def GLUc(jj, lo=0, hi=T):
            return BCbf[:, jj * T + lo:jj * T + hi]
        def HQc(k, lo=0, hi=QT):
            return BCbf[:, k * QT + lo:k * QT + hi]
        def MLPW(s):
            return BCbf[:, 8192 + s * 2048:8192 + (s + 1) * 2048]
        BCROW = BCf[0:1, 0:T]
        QZI = BCi[:, 1024:1536]
        WS = arena[:, O_WS:O_WS + 8192].bitcast(BF16)
        def WSS(s):
            return WS[:, s * 2048:(s + 1) * 2048]
        def DWS(s):
            return WS[:, s * 1024:(s + 1) * 1024]
        STAT = arena[:, O_STAT:O_STAT + TE * 4].bitcast(F32)
        def SR(r, lo=0, hi=TE):
            return STAT[r:r + 1, lo:hi]
        VEC = arena[:, O_VEC:O_VEC + 768].bitcast(F32)
        def V(c, n=1):
            return VEC[:, c:c + n]
        def V0(c):
            return VEC[0:1, c:c + 1]
        UPB = arena[:, O_UPB:O_UPB + 384].bitcast(F32)
        CW = arena[:, O_CW:O_CW + 256].bitcast(F32)
        EYE = arena[:, O_EYE:O_EYE + 512].bitcast(F32)
        ONESR = arena[0:1, O_ROW:O_ROW + 512].bitcast(F32)
        ONESR32 = arena[32:33, O_ROW:O_ROW + 512].bitcast(F32)
        ONESR64 = arena[64:65, O_ROW:O_ROW + 512].bitcast(F32)
        HEND = arena[:, O_HEND:O_HEND + 64].bitcast(F32)
        XHf = arena[:, O_XH:O_XH + 256].bitcast(F32)
        MISC = arena[:, O_MISC:O_MISC + 640]
        CCSB = MISC[:, 0:256].bitcast(F32)
        BNDR = CCSB
        SPR = MISC[:, 256:576].bitcast(F32)
        ONESB = MISC[:, 576:578].bitcast(BF16)
        def QW1S(s):
            return bav(s * 4096, 4096, BF16)
        def QW2S(s):
            return bav(16384 + s * 4096, 4096, BF16)
        RMAX = bav(24576, 1024, F32)
        SPRS = bav(25600, 1024, F32)
        QI32 = [bav(26624 + i * 1024, 1024, I32) for i in range(2)]
        EVS = bav(26624, 1024, F32)
        WQ32H = bav(28672, 4096, I32)
        ROWA = RMAX[0:1, 0:256]
        ROWB = SPRS[0:1, 0:256]
        ROWC = bav(26624, 1024, F32)[0:1, 0:256]
        def W1F(s):
            return bav(32768 + s * 8192, 8192, F32)
        def W2F(s):
            return bav(49152 + s * 8192, 8192, F32)

        lw_v = lw.rearrange("(k p) m -> p k m", p=128)
        rw_v = rw.rearrange("(k p) m -> p k m", p=128)
        gaw_v = gaw.rearrange("(k p) m -> p k m", p=128)
        gxw_v = gxw.rearrange("(k p) m -> p k m", p=128)
        ow_v = ow.rearrange("(k p) m -> p k m", p=128)
        u1_v = u1w.rearrange("(k p) m -> p k m", p=128)
        u2_v = u2w.rearrange("(k p) m -> p k m", p=128)
        dw_v = dwt.rearrange("(k p) m -> p k m", p=128)
        w1t_v = w1t.rearrange("(k p) m -> p k m", p=128)
        w1sl_v = w1sl.rearrange("(a p) m -> a p m", p=128)
        w2sl_v = w2sl.rearrange("(a p) m -> a p m", p=128)

        t_eye = em.op("sync", lambda e: e.dma_start(out=EYE[:], in_=eyev[:]), tag="eye")
        t_vec = em.op("sync", lambda e: e.dma_start(out=VEC[:], in_=vecs[:]), tag="vec")
        t_upb = em.op("sync", lambda e: e.dma_start(out=UPB[:], in_=upbv[:]), tag="upb")
        t_cw = em.op("sync", lambda e: e.dma_start(out=CW[:], in_=cwv[:]), tag="cw")
        t_ones = em.op("vector", lambda e: e.memset(ONESR[:], 1.0), tag="onesr")
        t_ones = em.op("vector", lambda e: e.memset(ONESR32[:], 1.0), after=[t_ones], tag="onesr32")
        t_ones = em.op("vector", lambda e: e.memset(ONESR64[:], 1.0), after=[t_ones], tag="onesr64")
        t_onesb = em.op("vector", lambda e: e.memset(ONESB[:], 1.0), tag="onesb")
        t_c = t_vec
        for col, val in ((C_M1, -1.0), (C_P1, 1.0), (C_SC1, 1.0 + EPS_RMS),
                         (C_B13, EPS_RMS * EPS_LN), (C_ELN, EPS_LN), (C_ERMS, EPS_RMS),
                         (C_2, 2.0), (C_HALF, 0.5), (C_INVD, 1.0 / D)):
            t_c = em.op("vector", lambda e, col=col, val=val: e.memset(V(col), val),
                        after=[t_vec], tag=f"c{col}")
        t_onesk = em.op("vector", lambda e: e.memset(V(VC_ONESK), 1.0), after=[t_vec],
                        tag="onesk")
        t_x = []
        for j in range(ND):
            t_x.append(em.op("sync", lambda e, j=j: e.dma_start(
                out=Xc(j), in_=xin[j * 128:(j + 1) * 128, HALO:TE]), tag=f"x{j}"))
            t_x.append(em.op("sync", lambda e, j=j: e.dma_start(
                out=XHf[:, j * HALO:(j + 1) * HALO],
                in_=xin[j * 128:(j + 1) * 128, 0:HALO]), tag=f"xh{j}"))
        t1 = em.op("vector", lambda e: e.tensor_single_scalar(SPR[:, 48:64], V(VC_SC, 16),
                                                              0.0, ALU.max),
                   after=[t_vec], tag="sp_relu")
        t2 = em.op("scalar", lambda e: e.activation(V(VC_SC, 16), V(VC_SC, 16), AF.Abs),
                   after=[t1], tag="sp_abs")
        t2 = em.op("scalar", lambda e: e.activation(V(VC_SC, 16), V(VC_SC, 16), AF.Exp,
                                                    scale=V(C_M1)),
                   after=[t2, t_c], tag="sp_exp")
        t2 = em.op("scalar", lambda e: e.activation(V(VC_SC, 16), V(VC_SC, 16), AF.Ln,
                                                    bias=V(C_P1)),
                   after=[t2], tag="sp_ln")
        t2 = em.op("vector", lambda e: e.tensor_add(V(VC_SC, 16), V(VC_SC, 16), SPR[:, 48:64]),
                   after=[t2], tag="sp_add")
        t_hsc = em.op("vector", lambda e: e.tensor_single_scalar(SPR[:, 32:48], V(VC_SC, 16),
                                                                 -0.5 * C_RGLRU, ALU.mult),
                      after=[t2], tag="halfsc")
        t_gabh = em.op("vector", lambda e: e.tensor_single_scalar(SPR[:, 0:16], V(VC_GAB, 16),
                                                                  0.5, ALU.mult),
                       after=[t_vec], tag="gabh")
        t_gxbh = em.op("vector", lambda e: e.tensor_single_scalar(SPR[:, 16:32], V(VC_GXB, 16),
                                                                  0.5, ALU.mult),
                       after=[t_vec], tag="gxbh")

        def xsrc(j, lo, hi):
            if hi <= 4:
                return XHf[:, j * HALO + lo:j * HALO + hi]
            return Xc(j, lo - 4, hi - 4)

        gmeta = [
            (0, 4, PS[2][0:1, 8:12], PS[2][0:1, 12:16], PS[2][:, 0:4], PS[2][:, 4:8]),
            (4, 516, PS[6][0:1, 0:512], PS[7][0:1, 0:512], PS[4][:], PS[6][:]),
            (516, 1028, PS[0][0:1, 0:512], PS[1][0:1, 0:512], PS[5][:], PS[7][:]),
        ]
        t_bc = []
        prev_sq_mm = None
        t_ln1_done = []
        for gi, (lo, hi, s1r, s2r, bmap, brap) in enumerate(gmeta):
            n = hi - lo
            MR = STAT[0:1, 0:n]
            VR = STAT[0:1, 516:516 + n]
            def s1fn(e, lo=lo, hi=hi, n=n, s1r=s1r):
                ins = None
                for j in range(ND):
                    ins = e.matmul(s1r, V(VC_ONESK), xsrc(j, lo, hi),
                                   start=(j == 0), stop=(j == ND - 1))
                return ins
            tmm = em.op("tensor", s1fn, after=t_x + [t_onesk] +
                        (t_bc[-2:] if gi > 0 else []), tag=f"s1_{gi}")
            sq = []
            for j in range(ND):
                sq.append(em.op("scalar", lambda e, j=j, lo=lo, hi=hi: e.activation(
                    CVc(j, 0, hi - lo), xsrc(j, lo, hi), AF.Square),
                    after=t_x + ([prev_sq_mm] if prev_sq_mm else []), tag=f"sq{gi}_{j}"))
            def s2fn(e, lo=lo, hi=hi, n=n, s2r=s2r):
                ins = None
                for j in range(ND):
                    ins = e.matmul(s2r, ONESB[:], CVc(j, 0, hi - lo),
                                   start=(j == 0), stop=(j == ND - 1))
                return ins
            tmm2 = em.op("tensor", s2fn, after=sq + [t_onesb] +
                         (t_bc[-2:] if gi > 0 else []), tag=f"s2_{gi}")
            prev_sq_mm = tmm2
            AR = STAT[32:33, 0:n]
            k1 = em.op("vector", lambda e, MR=MR, s1r=s1r: e.tensor_copy(MR, s1r),
                       after=[tmm, tmm2] + t_ln1_done[-2:], tag=f"mcp_{gi}")
            k1 = em.op("vector", lambda e, MR=MR: e.tensor_single_scalar(
                MR, MR, 1.0 / D, ALU.mult), after=[k1], tag=f"m_{gi}")
            k2 = em.op("scalar", lambda e, MR=MR, AR=AR: e.activation(AR, MR, AF.Square),
                       after=[k1], tag=f"msq_{gi}")
            k3 = em.op("vector", lambda e, VR=VR, s2r=s2r, AR=AR: e.scalar_tensor_tensor(
                VR, s2r, 1.0 / D, AR, ALU.mult, ALU.subtract),
                after=[k2], tag=f"var_{gi}")
            k4 = em.op("scalar", lambda e, VR=VR, AR=AR: e.activation(
                AR, VR, AF.Sqrt, scale=V0(C_SC1), bias=V0(C_B13)),
                after=[k3, t_c], tag=f"sd_{gi}")
            k5 = em.op("vector", lambda e, VR=VR, AR=AR: e.reciprocal(VR, AR),
                       after=[k4], tag=f"rp_{gi}")
            bcm = em.op("tensor", lambda e, bmap=bmap, MR=MR: e.matmul(
                bmap, ONESR[:], MR, start=True, stop=True),
                after=[k5, t_ones, k2], tag=f"bcm_{gi}")
            bcr = em.op("tensor", lambda e, brap=brap, VR=VR: e.matmul(
                brap, ONESR[:], VR, start=(gi != 0), stop=True,
                skip_group_check=True),
                after=[k5, bcm], tag=f"bcr_{gi}")
            t_bc += [bcm, bcr]
            t_ln1_done.append(bcr)
        t_xn_chunk = []
        prev = None
        for j in range(ND):
            sj = SCR[j % 2]
            for (lo, hi) in ((4, 516), (516, 1028), (0, 4)):
                n = hi - lo
                if lo == 4:
                    bm, br = PS[4], PS[6]
                elif lo == 516:
                    bm, br = PS[5], PS[7]
                else:
                    bm, br = PS[2][:, 0:4], PS[2][:, 4:8]
                ta = em.op("vector", lambda e, j=j, lo=lo, hi=hi, bm=bm, n=n, sj=sj:
                           e.tensor_tensor(sj[:, 0:n], xsrc(j, lo, hi),
                                           bm[:, 0:n] if hi > 4 else bm, ALU.subtract),
                           after=t_bc + ([prev] if prev else []), tag=f"xns{j}_{lo}")
                prev = em.op("vector", lambda e, j=j, lo=lo, hi=hi, br=br, n=n, sj=sj:
                             e.scalar_tensor_tensor(XNc(j, lo, hi), sj[:, 0:n], V(VC_N1W + j),
                                                    br[:, 0:n] if hi > 4 else br,
                                                    ALU.mult, ALU.mult),
                             after=[ta], tag=f"xnf{j}_{lo}")
            t_xn_chunk.append(prev)
            if KDBG and j == 0:
                t_out_dbg = [em.op("sync", lambda e: e.dma_start(out=dbg["dbg_xn"],
                             in_=XNc(0)), after=[prev], tag="dbg_xn")]
        t_xn = t_xn_chunk[-1]

        wring = Ring(2)

        def wload(wv, m, extra=(), tag=""):
            s, ftoks = wring.slot()
            tok = em.op("gpsimd", lambda e, wv=wv, m=m, s=s: e.dma_start(
                out=WSS(s).rearrange("p (k m) -> p k m", m=128),
                in_=wv[:, :, m * 128:(m + 1) * 128]),
                after=ftoks + list(extra), kind="dma", tag=f"wl_{tag}{m}")
            return s, tok

        mmb = Banks([0, 1, 2, 3])
        t_left_chunk = []
        for m in range(NH):
            s, tok = wload(lw_v, m, tag="lw")
            b0, bf0 = mmb.take()
            b1, bf1 = mmb.take()
            def mmfn(e, s=s, b0=b0, b1=b1):
                ins = None
                for k in range(ND):
                    w = WSS(s)[:, k * 128:(k + 1) * 128]
                    e.matmul(PS[b0][:], w, XNc(k, 4, 516),
                             start=(k == 0), stop=(k == ND - 1))
                    ins = e.matmul(PS[b1][:], w, XNc(k, 516, 1028),
                                   start=(k == 0), stop=(k == ND - 1), skip_group_check=True)
                return ins
            tm = em.op("tensor", mmfn, after=[tok, t_xn, bf0, bf1], tag=f"mml{m}")
            evs = []
            for h, b in ((0, b0), (1, b1)):
                tev = em.op("scalar", lambda e, m=m, h=h, b=b: e.activation(
                    Lc(m, h * 512, (h + 1) * 512), PS[b][:], AF.Gelu, bias=V(VC_LB + m)),
                    after=[tm, t_vec], tag=f"gel{m}_{h}")
                mmb.done(b, tev)
                evs.append(tev)
            wring.consumed(s, [evs[-1]])
            t_left_chunk.append(evs[-1])
            if KDBG and m == 0:
                t_out_dbg.append(em.op("sync", lambda e: e.dma_start(out=dbg["dbg_left"],
                                 in_=Lc(0)), after=[evs[-1]], tag="dbg_left"))

        t_conv_chunk = []
        for m in range(NH):
            s, tok = wload(rw_v, m, tag="rw")
            b0, bf0 = mmb.take()
            b1, bf1 = mmb.take()
            b2, bf2 = mmb.take()
            def mmfn(e, s=s, b0=b0, b1=b1, b2=b2):
                ins = None
                for k in range(ND):
                    w = WSS(s)[:, k * 128:(k + 1) * 128]
                    e.matmul(PS[b0][:], w, XNc(k, 4, 516),
                             start=(k == 0), stop=(k == ND - 1))
                    e.matmul(PS[b1][:], w, XNc(k, 516, 1028),
                             start=(k == 0), stop=(k == ND - 1), skip_group_check=True)
                    ins = e.matmul(PS[b2][:, 0:4], w, XNc(k, 0, 4),
                                   start=(k == 0), stop=(k == ND - 1), skip_group_check=True)
                return ins
            tm = em.op("tensor", mmfn, after=[tok, t_xn, bf0, bf1, bf2], tag=f"mmr{m}")
            evs = []
            for (lo, hi, b) in ((4, 516, b0), (516, 1028, b1), (0, 4, b2)):
                n = hi - lo
                if lo == 0:
                    tev = em.op("vector", lambda e, m=m, b=b: e.tensor_scalar(
                        RP[:, 0:4], PS[b][:, 0:4], V(VC_RB + m), V(VC_MASK),
                        ALU.add, ALU.mult), after=[tm, t_vec], tag=f"rpev{m}_h")
                else:
                    tev = em.op("vector", lambda e, m=m, lo=lo, hi=hi, b=b, n=n:
                                e.tensor_single_scalar(RP[:, lo:hi], PS[b][:, 0:n],
                                                       V(VC_RB + m), ALU.add),
                                after=[tm, t_vec], tag=f"rpev{m}_{lo}")
                mmb.done(b, tev)
                evs.append(tev)
            wring.consumed(s, [evs[-1]])
            acc = SCR[5]
            tcv = em.op("vector", lambda e, m=m, acc=acc: e.tensor_scalar(
                acc[:, 0:T], RP[:, 1:1 + T], CW[:, m:m + 1], V(VC_CB + m),
                ALU.mult, ALU.add), after=evs + [t_cw], tag=f"cv{m}_0")
            for k in (1, 2):
                tcv = em.op("vector", lambda e, m=m, k=k, acc=acc: e.scalar_tensor_tensor(
                    acc[:, 0:T], RP[:, 1 + k:1 + k + T], CW[:, k * 16 + m:k * 16 + m + 1],
                    acc[:, 0:T], ALU.mult, ALU.add), after=[tcv], tag=f"cv{m}_{k}")
            tcv = em.op("vector", lambda e, m=m, acc=acc: e.scalar_tensor_tensor(
                CVc(m), RP[:, 4:4 + T], CW[:, 48 + m:48 + m + 1],
                acc[:, 0:T], ALU.mult, ALU.add), after=[tcv], tag=f"cv{m}_3")
            t_conv_chunk.append(tcv)
            if KDBG and m == 0:
                t_out_dbg.append(em.op("sync", lambda e: e.dma_start(out=dbg["dbg_conv"],
                                 in_=CVc(0)), after=[tcv], tag="dbg_conv"))

        tz = em.op("vector", lambda e: e.memset(SPR[:, 64:66], 0.0), tag="mz")
        mslot = [[tz], [tz]]
        t_acc = tz
        for wi, (wv, npc, blk) in enumerate(((w1sl_v, 16, 1), (w2sl_v, 16, 4))):
            acc = SPR[:, 64 + wi:65 + wi]
            for i in range(npc):
                s = i % 2
                tl = em.op("sync", lambda e, wv=wv, i=i, s=s, blk=blk: e.dma_start(
                    out=SCR[2 + s][:, 0:1024].rearrange("p (a m) -> p a m", a=blk),
                    in_=wv[i * blk:(i + 1) * blk].rearrange("a p m -> p a m")),
                    after=mslot[s], tag=f"wsl{wi}_{i}")
                ta = em.op("vector", lambda e, s=s: e.reduce_sum(
                    SPR[:, 66:67], SCR[2 + s][:, 0:1024], AX.X, apply_absolute_value=True),
                    after=[tl, t_acc], tag=f"wabs{wi}_{i}")
                t_acc = em.op("vector", lambda e, acc=acc: e.tensor_add(acc, acc, SPR[:, 66:67]),
                              after=[ta], tag=f"wacc{wi}_{i}")
                mslot[s] = [t_acc]
        tm = em.op("tensor", lambda e: (
            e.matmul(PS[6][0:1, 0:1], SPR[:, 64:65], V(VC_ONESK), start=True, stop=True),
            e.matmul(PS[6][0:1, 1:2], SPR[:, 65:66], V(VC_ONESK), start=False, stop=True,
                     skip_group_check=True))[-1],
            after=[t_acc, t_onesk, t_xn], tag="mm_ws")
        tcpm = em.op("vector", lambda e: e.tensor_copy(SPR[0:1, 68:70], PS[6][0:1, 0:2]),
                     after=[tm], tag="ws_cp")
        tdm = em.op("sync", lambda e: e.dma_start(out=ar_in[:], in_=SPR[0:1, 68:70]),
                    after=[tcpm], tag="ar_in")
        tcl = em.op("gpsimd", lambda e: e.collective_compute(
            "AllReduce", ALU.add, [[0, 1, 2, 3, 4, 5, 6, 7]], [ar_in.opt()], [ar_out.opt()]),
            after=[tdm], kind="col", tag="ar_mean")
        tbkm = em.op("sync", lambda e: e.dma_start(out=SPR[0:1, 68:70], in_=ar_out[:]),
                     after=[tcl], tag="ar_bk")
        tmn = em.op("vector", lambda e: e.tensor_scalar(
            SPR[0:1, 70:72], SPR[0:1, 68:70], 1.0 / (D * FF), EPS_Q, ALU.mult, ALU.max),
            after=[tbkm], tag="mean")
        trc = em.op("vector", lambda e: e.reciprocal(SPR[0:1, 72:74], SPR[0:1, 70:72]),
                    after=[tmn], tag="wsv")
        tsc2 = em.op("vector", lambda e: e.tensor_single_scalar(
            SPR[0:1, 74:76], SPR[0:1, 70:72], 1.0 / 127.0, ALU.mult),
            after=[tmn], tag="mdiv")
        def bwsfn(e):
            e.matmul(PS[6][0:1 if False else slice(None), 0:2] if False else PS[6][:, 0:2],
                     ONESR[:], SPR[0:1, 72:74], start=True, stop=True)
            return e.matmul(PS[6][:, 2:4], ONESR[:], SPR[0:1, 74:76], start=False, stop=True,
                            skip_group_check=True)
        tbws = em.op("tensor", bwsfn, after=[trc, tsc2, t_ones, tcpm, t_xn], tag="bws")
        t_ws1 = em.op("vector", lambda e: e.tensor_copy(V(VC_WS1), PS[6][:, 0:1]),
                      after=[tbws], tag="ws1cp")
        t_ws2 = em.op("vector", lambda e: e.tensor_copy(V(VC_WS2), PS[6][:, 1:2]),
                      after=[tbws], tag="ws2cp")
        t_w1scl = em.op("vector", lambda e: e.tensor_copy(V(VC_W1SCL), PS[6][:, 2:3]),
                        after=[tbws], tag="w1scl")
        t_w2scl = em.op("vector", lambda e: e.tensor_copy(V(VC_W2SCL), PS[6][:, 3:4]),
                        after=[tbws], tag="w2scl")
        t_ps6m = t_w2scl

        t_hend = []
        t_q_chunk = []
        t_lr_chunk = []
        sameparity = {}
        scan_prev = None
        for j in range(ND):
            s_ga, tok_ga = wload(gaw_v, j, extra=(t_conv_chunk if j == 0 else []), tag="ga")
            s_gx, tok_gx = wload(gxw_v, j, tag="gx")
            p = j % 2
            s0, s1, s2, s3 = (SCR[p * 4 + i] for i in range(4))
            old = sameparity.get(p, {})
            ba0, bfa0 = mmb.take()
            ba1, bfa1 = mmb.take()
            def mmafn(e, s=s_ga, b0=ba0, b1=ba1):
                ins = None
                for k in range(ND):
                    w = WSS(s)[:, k * 128:(k + 1) * 128]
                    e.matmul(PS[b0][:], w, CVc(k, 0, 512),
                             start=(k == 0), stop=(k == ND - 1))
                    ins = e.matmul(PS[b1][:], w, CVc(k, 512, 1024),
                                   start=(k == 0), stop=(k == ND - 1), skip_group_check=True)
                return ins
            tmg = em.op("tensor", mmafn, after=[tok_ga, bfa0, bfa1] + t_conv_chunk,
                        tag=f"mga{j}")
            tr = []
            for h, b in ((0, ba0), (1, ba1)):
                tev = em.op("scalar", lambda e, h=h, b=b, s0=s0, j=j: e.activation(
                    s0[:, h * 512:(h + 1) * 512], PS[b][:], AF.Tanh,
                    scale=V(C_HALF), bias=SPR[:, 0 + j:1 + j]),
                    after=[tmg, t_gabh, t_c, old.get("s0")], tag=f"thr{j}_{h}")
                mmb.done(b, tev)
                tr.append(tev)
            wring.consumed(s_ga, [tr[-1]])
            if KDBG and j == 0:
                tr = tr + [em.op("sync", lambda e, s0=s0: e.dma_start(
                    out=dbg["dbg_th"], in_=s0[:]), after=tr, tag="dbg_th"),
                    em.op("sync", lambda e: e.dma_start(
                        out=dbg["dbg_spr"], in_=SPR[:]), after=tr + [t_hsc], tag="dbg_spr")]
                t_out_dbg += tr[-2:]
            tg = em.op("vector", lambda e, s0=s0, j=j: e.tensor_scalar(
                s0[:], s0[:], SPR[:, 32 + j:33 + j], SPR[:, 32 + j:33 + j], ALU.mult, ALU.add),
                after=tr + [t_hsc], tag=f"loga{j}")
            if KDBG and j == 0:
                tdmp = em.op("sync", lambda e, s0=s0: e.dma_start(
                    out=dbg["dbg_loga"], in_=s0[:]), after=[tg], tag="dbg_loga")
                tg = tdmp
                t_out_dbg.append(tdmp)
            te2 = em.op("scalar", lambda e, s0=s0, s1=s1: e.activation(
                s1[:], s0[:], AF.Exp, scale=V(C_2)),
                after=[tg, old.get("s1")], tag=f"e2_{j}")
            tt1 = em.op("vector", lambda e, s1=s1: e.tensor_scalar(
                s1[:], s1[:], -1.0, 1.0, ALU.mult, ALU.add), after=[te2], tag=f"ome2_{j}")
            tt1 = em.op("vector", lambda e, s1=s1: e.tensor_single_scalar(
                s1[:], s1[:], 0.0, ALU.max), after=[tt1], tag=f"clamp_{j}")
            ta_ = em.op("scalar", lambda e, s0=s0: e.activation(s0[:], s0[:], AF.Exp),
                        after=[tg, te2], tag=f"a_{j}")
            tmu = em.op("scalar", lambda e, s1=s1: e.activation(s1[:], s1[:], AF.Sqrt),
                        after=[tt1], tag=f"mult{j}")
            if KDBG and j == 0:
                tmu = em.op("sync", lambda e, s1=s1: e.dma_start(
                    out=dbg["dbg_mult"], in_=s1[:]), after=[tmu], tag="dbg_mult")
                t_out_dbg.append(tmu)
            bx0, bfx0 = mmb.take()
            bx1, bfx1 = mmb.take()
            def mmxfn(e, s=s_gx, b0=bx0, b1=bx1):
                ins = None
                for k in range(ND):
                    w = WSS(s)[:, k * 128:(k + 1) * 128]
                    e.matmul(PS[b0][:], w, CVc(k, 0, 512),
                             start=(k == 0), stop=(k == ND - 1))
                    ins = e.matmul(PS[b1][:], w, CVc(k, 512, 1024),
                                   start=(k == 0), stop=(k == ND - 1), skip_group_check=True)
                return ins
            tmg = em.op("tensor", mmxfn, after=[tok_gx, bfx0, bfx1], tag=f"mgx{j}")
            ti = []
            for h, b in ((0, bx0), (1, bx1)):
                tev = em.op("scalar", lambda e, h=h, b=b, s2=s2, j=j: e.activation(
                    s2[:, h * 512:(h + 1) * 512], PS[b][:], AF.Tanh,
                    scale=V(C_HALF), bias=SPR[:, 16 + j:17 + j]),
                    after=[tmg, t_gxbh, old.get("s2"),
                           (t_acc if j == 0 else None)], tag=f"thi{j}_{h}")
                mmb.done(b, tev)
                ti.append(tev)
            wring.consumed(s_gx, [ti[-1]])
            tb1 = em.op("vector", lambda e, s2=s2: e.tensor_scalar(
                s2[:], s2[:], 0.5, 0.5, ALU.mult, ALU.add), after=ti, tag=f"ig{j}")
            if KDBG and j == 0:
                tb1 = em.op("sync", lambda e, s2=s2: e.dma_start(
                    out=dbg["dbg_i"], in_=s2[:]), after=[tb1], tag="dbg_i")
                t_out_dbg.append(tb1)
            tb2 = em.op("vector", lambda e, s2=s2, j=j: e.tensor_tensor(
                s2[:], s2[:], CVc(j), ALU.mult), after=[tb1], tag=f"bx{j}")
            tb3 = em.op("vector", lambda e, s2=s2, s1=s1: e.tensor_tensor(
                s2[:], s2[:], s1[:], ALU.mult), after=[tb2, tmu], tag=f"bm{j}")
            if KDBG and j == 0:
                tb3 = em.op("sync", lambda e, s2=s2: e.dma_start(
                    out=dbg["dbg_b"], in_=s2[:]), after=[tb3], tag="dbg_b")
                ta_ = em.op("sync", lambda e, s0=s0: e.dma_start(
                    out=dbg["dbg_a"], in_=s0[:]), after=[ta_], tag="dbg_a")
                t_out_dbg += [tb3, ta_]
            th = em.op("vector", lambda e, s0=s0, s2=s2, s3=s3: e.tensor_tensor_scan(
                s3[:], s0[:], s2[:], 0.0, ALU.mult, ALU.add),
                after=[ta_, tb3, scan_prev, old.get("s3"),
                       (t_acc if j == 0 else None)], tag=f"scanh{j}")
            tp = em.op("vector", lambda e, s0=s0, s1=s1: e.tensor_tensor_scan(
                s1[:], s0[:], s0[:], 1.0, ALU.mult, ALU.bypass),
                after=[th], tag=f"scanp{j}")
            scan_prev = tp
            tq = em.op("vector", lambda e, j=j, s1=s1: e.tensor_tensor(
                Qc(j), Lc(j), s1[:], ALU.mult), after=[tp, t_left_chunk[j]], tag=f"q{j}")
            tlr = em.op("vector", lambda e, j=j, s3=s3: e.tensor_tensor(
                Lc(j), Lc(j), s3[:], ALU.mult), after=[tq, th], tag=f"lr{j}")
            the = em.op("vector", lambda e, j=j, s3=s3: e.tensor_copy(
                HEND[:, j:j + 1], s3[:, 1023:1024]), after=[tlr], tag=f"he{j}")
            if KDBG and j == 0:
                t_out_dbg.append(em.op("sync", lambda e, s3=s3: e.dma_start(out=dbg["dbg_h"],
                                 in_=s3[:]), after=[the], tag="dbg_h"))
                t_out_dbg.append(em.op("sync", lambda e: e.dma_start(out=dbg["dbg_stat"],
                                 in_=STAT[0:1, 0:TE]), after=[the], tag="dbg_stat"))
            t_hend.append(the)
            t_q_chunk.append(tq)
            t_lr_chunk.append(tlr)
            sameparity[p] = {"s0": th, "s1": tq, "s2": th, "s3": the}

        tcc = None
        for bi in range(4):
            tcc = em.op("vector", lambda e, bi=bi: e.tensor_single_scalar(
                CCSB[:, bi * 16:(bi + 1) * 16], HEND[:], V(VC_SEL + bi), ALU.mult),
                after=t_hend + ([tcc] if tcc else []), tag=f"ccsb{bi}")
        tdm2 = em.op("sync", lambda e: e.dma_start(out=cc_in[:], in_=CCSB[:]),
                     after=[tcc], tag="cc_in")
        tcl2 = em.op("gpsimd", lambda e: e.collective_compute(
            "AllReduce", ALU.add, [[0, 1, 2, 3, 4, 5, 6, 7]], [cc_in.opt()], [cc_out.opt()]),
            after=[tdm2], kind="col", tag="ar_h")
        tbk2 = em.op("sync", lambda e: e.dma_start(out=BNDR[:], in_=cc_out[:]),
                     after=[tcl2], tag="cc_bk")
        tb_ = em.op("vector", lambda e: e.tensor_single_scalar(
            SPR[:, 48:64], BNDR[:, 0:16], V(VC_RSEL + 0), ALU.mult), after=[tbk2], tag="bnd0")
        for bi in range(1, 4):
            tb_ = em.op("vector", lambda e, bi=bi: e.scalar_tensor_tensor(
                SPR[:, 48:64], BNDR[:, bi * 16:(bi + 1) * 16], V(VC_RSEL + bi),
                SPR[:, 48:64], ALU.mult, ALU.add), after=[tb_], tag=f"bnd{bi}")
        t_lr_fin = []
        for j in range(ND):
            t_lr_fin.append(em.op("vector", lambda e, j=j: e.scalar_tensor_tensor(
                Lc(j), Qc(j), SPR[:, 48 + j:49 + j], Lc(j), ALU.mult, ALU.add),
                after=[tb_, t_q_chunk[j], t_lr_chunk[j]], tag=f"corr{j}"))

        t_bco = []
        prev_s2 = None
        for hh in range(2):
            lo, hi = hh * 512, (hh + 1) * 512
            s1r = PS[2][0:1, 0:512]
            s2r = PS[3][0:1, 0:512]
            MR = STAT[0:1, 0:512]
            VR = STAT[0:1, 516:1028]
            def s1fn(e, lo=lo, hi=hi, s1r=s1r):
                ins = None
                for j in range(ND):
                    ins = e.matmul(s1r, V(VC_ONESK), Xc(j, lo, hi),
                                   start=(j == 0), stop=(j == ND - 1))
                return ins
            tmm = em.op("tensor", s1fn,
                        after=[t_lr_fin[-1]] + (t_bco[-2:] if hh else []), tag=f"ys1_{hh}")
            sq = []
            for j in range(ND):
                sq.append(em.op("scalar", lambda e, j=j, lo=lo, hi=hi: e.activation(
                    BB[:, j * 512:(j + 1) * 512], Xc(j, lo, hi), AF.Square),
                    after=[t_lr_fin[-1]] + ([prev_s2] if prev_s2 else []), tag=f"ysq{hh}_{j}"))
            def s2fn(e, s2r=s2r):
                ins = None
                for j in range(ND):
                    ins = e.matmul(s2r, ONESB[:], BB[:, j * 512:(j + 1) * 512],
                                   start=(j == 0), stop=(j == ND - 1))
                return ins
            tmm2 = em.op("tensor", s2fn, after=sq + (t_bco[-2:] if hh else []), tag=f"ys2_{hh}")
            prev_s2 = tmm2
            AR = STAT[32:33, 0:512]
            k1 = em.op("vector", lambda e, MR=MR, s1r=s1r: e.tensor_copy(MR, s1r),
                       after=[tmm, tmm2] + (t_bco[-2:] if hh else t_bc[-2:]), tag=f"ymc_{hh}")
            k1 = em.op("vector", lambda e, MR=MR: e.tensor_single_scalar(
                MR, MR, 1.0 / D, ALU.mult), after=[k1], tag=f"ym_{hh}")
            k2 = em.op("scalar", lambda e, MR=MR, AR=AR: e.activation(
                AR, MR, AF.Square), after=[k1], tag=f"ymsq_{hh}")
            k3 = em.op("vector", lambda e, VR=VR, s2r=s2r, AR=AR: e.scalar_tensor_tensor(
                VR[0:1, 0:512], s2r, 1.0 / D, AR, ALU.mult, ALU.subtract),
                after=[k2], tag=f"yvar_{hh}")
            k4 = em.op("scalar", lambda e, VR=VR, AR=AR: e.activation(
                AR, VR[0:1, 0:512], AF.Sqrt, bias=V0(C_ELN)),
                after=[k3], tag=f"ysd_{hh}")
            k5 = em.op("vector", lambda e, VR=VR, AR=AR: e.reciprocal(VR[0:1, 0:512], AR),
                       after=[k4], tag=f"yr_{hh}")
            bcm = em.op("tensor", lambda e, hh=hh, MR=MR: e.matmul(
                PS[4 + hh][:], ONESR[:], MR, start=True, stop=True),
                after=[k5, (t_ps6m if hh == 0 else None)], tag=f"ybcm_{hh}")
            bcr = em.op("tensor", lambda e, hh=hh, VR=VR: e.matmul(
                PS[6 + hh][:], ONESR[:], VR[0:1, 0:512], start=True, stop=True),
                after=[k5, bcm], tag=f"ybcr_{hh}")
            t_bco += [bcm, bcr]
        t_x1_chunk = []
        t_x2p_chunk = []
        prevy = None
        YSCR = arena[:, O_BB + 16384:O_BB + 18432].bitcast(F32)
        for m in range(ND):
            s, tok = wload(ow_v, m, tag="ow")
            evs = []
            bo0, bof0 = mmb.take()
            bo1, bof1 = mmb.take()
            def mmofn(e, s=s, b0=bo0, b1=bo1):
                ins = None
                for k in range(NH):
                    w = WSS(s)[:, k * 128:(k + 1) * 128]
                    e.matmul(PS[b0][:], w, Lc(k, 0, 512),
                             start=(k == 0), stop=(k == NH - 1))
                    ins = e.matmul(PS[b1][:], w, Lc(k, 512, 1024),
                                   start=(k == 0), stop=(k == NH - 1), skip_group_check=True)
                return ins
            tmo_g = em.op("tensor", mmofn,
                          after=[tok, bof0, bof1] + t_lr_fin + (t_bco if m < 2 else []),
                          tag=f"mmo{m}")
            for h in range(2):
                b = bo0 if h == 0 else bo1
                bm = PS[4] if h == 0 else PS[5]
                br = PS[6] if h == 0 else PS[7]
                tmo = tmo_g
                ty1 = em.op("vector", lambda e, m=m, h=h, bm=bm: e.tensor_tensor(
                    X1c(m, h * 512, (h + 1) * 512), Xc(m, h * 512, (h + 1) * 512), bm[:],
                    ALU.subtract),
                    after=t_bco + ([prevy] if prevy else []), tag=f"y1_{m}_{h}")
                ty2 = em.op("vector", lambda e, m=m, h=h, br=br: e.tensor_tensor(
                    X1c(m, h * 512, (h + 1) * 512), X1c(m, h * 512, (h + 1) * 512), br[:],
                    ALU.mult), after=[ty1], tag=f"y2_{m}_{h}")
                tx1 = em.op("vector", lambda e, m=m, h=h, b=b: e.scalar_tensor_tensor(
                    X1c(m, h * 512, (h + 1) * 512), PS[b][:], V(VC_OB + m),
                    X1c(m, h * 512, (h + 1) * 512),
                    ALU.add, ALU.add), after=[tmo, ty2], tag=f"x1_{m}_{h}")
                prevy = tx1
                mmb.done(b, tx1)
                tx2 = em.op("vector", lambda e, m=m, h=h: e.tensor_tensor(
                    Xc(m, h * 512, (h + 1) * 512), Xc(m, h * 512, (h + 1) * 512),
                    X1c(m, h * 512, (h + 1) * 512), ALU.add), after=[tx1], tag=f"x2p{m}_{h}")
                evs.append(tx2)
                t_x1_chunk.append(tx1)
                t_x2p_chunk.append(tx2)
            wring.consumed(s, [evs[-1]])
            if KDBG and m == 0:
                t_out_dbg.append(em.op("sync", lambda e: e.dma_start(out=dbg["dbg_x1"],
                                 in_=X1c(0)), after=[evs[-1]], tag="dbg_x1"))

        t_bcq = []
        prev_s2b = None
        for hh in range(2):
            lo, hi = hh * 512, (hh + 1) * 512
            s2r = PS[2][0:1, 0:512] if hh == 0 else PS[3][0:1, 0:512]
            VR = STAT[0:1, 516:1028]
            sq = []
            for j in range(ND):
                sq.append(em.op("scalar", lambda e, j=j, lo=lo, hi=hi: e.activation(
                    BB[:, 8192 + j * 512:8192 + (j + 1) * 512], X1c(j, lo, hi), AF.Square),
                    after=[t_x1_chunk[2 * j + hh]] + ([prev_s2b] if prev_s2b else []),
                    tag=f"sq2_{hh}_{j}"))
            def s2fn(e, s2r=s2r):
                ins = None
                for j in range(ND):
                    ins = e.matmul(s2r, ONESB[:], BB[:, 8192 + j * 512:8192 + (j + 1) * 512],
                                   start=(j == 0), stop=(j == ND - 1))
                return ins
            tms = em.op("tensor", s2fn,
                        after=sq + ([t_x1_chunk[-1]] if hh == 0 else []), tag=f"rms2_{hh}")
            prev_s2b = tms
            AR = STAT[32:33, 0:512]
            k4 = em.op("scalar", lambda e, AR=AR, s2r=s2r: e.activation(
                AR, s2r, AF.Sqrt, scale=V0(C_INVD), bias=V0(C_ERMS)),
                after=[tms, (t_bco[-1] if hh == 0 else None)], tag=f"r2sd_{hh}")
            k5 = em.op("vector", lambda e, VR=VR, AR=AR: e.reciprocal(VR[0:1, 0:512], AR),
                       after=[k4], tag=f"r2r_{hh}")
            bcq = em.op("tensor", lambda e, hh=hh, VR=VR: e.matmul(
                PS[4 + hh][:], ONESR[:], VR[0:1, 0:512], start=True, stop=True),
                after=[k5] + t_x2p_chunk, tag=f"bcq_{hh}")
            t_bcq.append(bcq)
        bcq1, bcq2 = t_bcq
        t_x1n_chunk = []
        for j in range(ND):
            tk = None
            for h, bq in ((0, PS[4]), (1, PS[5])):
                tk = em.op("vector", lambda e, j=j, h=h, bq=bq: e.scalar_tensor_tensor(
                    X1Nc(j, h * 512, (h + 1) * 512), X1c(j, h * 512, (h + 1) * 512),
                    V(VC_N2W + j), bq[:], ALU.mult, ALU.mult),
                    after=[bcq1, bcq2, t_lr_fin[j], t_q_chunk[j]], tag=f"x1n{j}_{h}")
            t_x1n_chunk.append(tk)
            if KDBG and j == 0:
                t_out_dbg.append(em.op("sync", lambda e, tk=tk: e.dma_start(out=dbg["dbg_x1n"],
                                 in_=X1Nc(0)), after=[tk], tag="dbg_x1n"))
        t_x1n = t_x1n_chunk[-1]

        mring = Ring(4)
        dring = Ring(4)
        dring.free[0] = list(wring.free[0])
        dring.free[1] = list(wring.free[0])
        dring.free[2] = list(wring.free[1])
        dring.free[3] = list(wring.free[1])
        upbk = Banks([0, 1, 2, 3], free=mmb.free)
        dnbk = Banks([6, 7], free={6: t_x1_chunk[-1], 7: t_x1_chunk[-1]})
        t_dacc = {}
        gelu_rot = [0]
        glu_pend = None
        gelu_free = {4: t_x1n_chunk[-1], 5: t_x1n_chunk[-1]}
        NG, GS = 6, 8
        t_glu_last = []
        for g in range(NG):
            glu_toks = []
            for mi in range(GS):
                m = g * GS + mi
                sA, fA = mring.slot()
                tl1 = em.op("gpsimd", lambda e, m=m, s=sA: e.dma_start(
                    out=MLPW(s).rearrange("p (k m) -> p k m", m=128),
                    in_=u1_v[:, :, m * 128:(m + 1) * 128]),
                    after=fA + ([t_x1_chunk[-1]] if m < 2 else []), kind="dma", tag=f"u1l{m}")
                sB, fB = mring.slot()
                tl2 = em.op("gpsimd", lambda e, m=m, s=sB: e.dma_start(
                    out=MLPW(s).rearrange("p (k m) -> p k m", m=128),
                    in_=u2_v[:, :, m * 128:(m + 1) * 128]),
                    after=fB, kind="dma", tag=f"u2l{m}")
                last = None
                for h in range(2):
                    b1, bf1 = upbk.take()
                    def mm1fn(e, s=sA, h=h, b=b1):
                        ins = None
                        for k in range(ND):
                            ins = e.matmul(PS[b][:], MLPW(s)[:, k * 128:(k + 1) * 128],
                                           X1Nc(k, h * 512, (h + 1) * 512),
                                           start=(k == 0), stop=(k == ND - 1))
                        return ins
                    tm1 = em.op("tensor", mm1fn, after=[tl1, t_x1n, bf1], tag=f"u1m{m}_{h}")
                    if KDBG and m == 0 and h == 0:
                        tm1 = em.op("sync", lambda e: e.dma_start(
                            out=dbg["dbg_x1n4"], in_=X1Nc(4)), after=[tm1], tag="dbg_x1n4")
                        t_out_dbg.append(tm1)
                    gb = 4 + (gelu_rot[0] % 2)
                    gelu_rot[0] += 1
                    tg1 = em.op("scalar", lambda e, m=m, b=b1, gb=gb: e.activation(
                        PS[gb][:], PS[b][:], AF.Gelu,
                        bias=UPB[:, m:m + 1]), after=[tm1, t_upb, gelu_free[gb]],
                        tag=f"glg{m}_{h}")
                    upbk.done(b1, tg1)
                    b2, bf2 = upbk.take()
                    def mm2fn(e, s=sB, h=h, b=b2):
                        ins = None
                        for k in range(ND):
                            ins = e.matmul(PS[b][:], MLPW(s)[:, k * 128:(k + 1) * 128],
                                           X1Nc(k, h * 512, (h + 1) * 512),
                                           start=(k == 0), stop=(k == ND - 1))
                        return ins
                    tm2 = em.op("tensor", mm2fn, after=[tl2, t_x1n, bf2], tag=f"u2m{m}_{h}")
                    tu2 = em.op("vector", lambda e, mi=mi, m=m, h=h, b=b2:
                                e.tensor_single_scalar(
                                    GLUc(mi, h * 512, (h + 1) * 512), PS[b][:],
                                    UPB[:, 48 + m:48 + m + 1], ALU.add),
                                after=[tm2], tag=f"u2b{m}_{h}")
                    upbk.done(b2, tu2)
                    if glu_pend is not None:
                        pmi, ph, pgb, ptu2, ptg1, pm = glu_pend
                        last = em.op("vector", lambda e, pmi=pmi, ph=ph, pgb=pgb:
                                     e.tensor_tensor(
                                         GLUc(pmi, ph * 512, (ph + 1) * 512),
                                         GLUc(pmi, ph * 512, (ph + 1) * 512),
                                         PS[pgb][:], ALU.mult),
                                     after=[ptu2, ptg1, tg1], tag=f"glu{pm}_{ph}")
                        gelu_free[pgb] = last
                        glu_toks.append(last)
                    glu_pend = (mi, h, gb, tu2, tg1, m)
                mring.consumed(sA, [last])
                mring.consumed(sB, [last])
            if KDBG and g == 0:
                tdbg = em.op("sync", lambda e: e.dma_start(
                    out=dbg["dbg_glu"], in_=GLUc(0)), after=glu_toks, tag="dbg_glu")
                glu_toks = glu_toks + [tdbg]
                t_out_dbg.append(tdbg)
            if glu_pend is not None:
                pmi, ph, pgb, ptu2, ptg1, pm = glu_pend
                tdmy = em.op("scalar", lambda e: e.activation(
                    SPR[32:33, 78:79], SPR[32:33, 78:79], AF.Copy),
                    after=[ptg1], tag=f"gfl{g}")
                last = em.op("vector", lambda e, pmi=pmi, ph=ph, pgb=pgb: e.tensor_tensor(
                    GLUc(pmi, ph * 512, (ph + 1) * 512),
                    GLUc(pmi, ph * 512, (ph + 1) * 512), PS[pgb][:], ALU.mult),
                    after=[ptu2, ptg1, tdmy], tag=f"glu{pm}_{ph}f")
                gelu_free[pgb] = last
                glu_toks.append(last)
                glu_pend = None
            for o in range(ND):
                sD, fD = dring.slot()
                tld = em.op("gpsimd", lambda e, g=g, o=o, s=sD: e.dma_start(
                    out=DWS(s).rearrange("p (k m) -> p k m", m=128),
                    in_=dw_v[:, g * GS:(g + 1) * GS, o * 128:(o + 1) * 128]),
                    after=fD + (glu_toks if o < 4 else []) +
                    ([t_x2p_chunk[-1]] if g == 0 and o < 4 else []),
                    kind="dma", tag=f"dwl{g}_{o}")
                evs = []
                for h in range(2):
                    b, bfree = dnbk.take()
                    def mmdfn(e, s=sD, h=h, b=b):
                        ins = None
                        for kk in range(GS):
                            ins = e.matmul(PS[b][:], DWS(s)[:, kk * 128:(kk + 1) * 128],
                                           GLUc(kk, h * 512, (h + 1) * 512),
                                           start=(kk == 0), stop=(kk == GS - 1))
                        return ins
                    tmd = em.op("tensor", mmdfn, after=[tld, bfree] + glu_toks,
                                tag=f"dm{g}_{o}_{h}")
                    if g == 0:
                        tev = em.op("vector", lambda e, o=o, h=h, b=b: e.tensor_copy(
                            X1c(o, h * 512, (h + 1) * 512), PS[b][:]),
                            after=[tmd, t_x1n_chunk[o]], tag=f"da{g}_{o}_{h}")
                    else:
                        tev = em.op("vector", lambda e, o=o, h=h, b=b: e.tensor_tensor(
                            X1c(o, h * 512, (h + 1) * 512), X1c(o, h * 512, (h + 1) * 512),
                            PS[b][:], ALU.add), after=[tmd, t_dacc[(o, h)]],
                            tag=f"da{g}_{o}_{h}")
                    dnbk.done(b, tev)
                    t_dacc[(o, h)] = tev
                    evs.append(tev)
                dring.consumed(sD, [evs[-1]])
            t_glu_last = glu_toks
        if KDBG:
            tdd = em.op("sync", lambda e: e.dma_start(
                out=dbg["dbg_dacc"], in_=X1c(0)), after=[t_dacc[(0, 0)], t_dacc[(0, 1)]],
                tag="dbg_dacc")
            t_dacc[(0, 0)] = tdd
            t_out_dbg.append(tdd)
        t_x2_chunk = []
        for j in range(ND):
            t_x2_chunk.append(em.op("vector", lambda e, j=j: e.scalar_tensor_tensor(
                Xc(j), X1c(j), V(VC_DB + j), Xc(j), ALU.add, ALU.add),
                after=[t_dacc[(j, 0)], t_dacc[(j, 1)], t_x2p_chunk[2 * j + 1]], tag=f"x2_{j}"))

        if KDBG:
            t_out_dbg.append(em.op("sync", lambda e: e.dma_start(out=dbg["dbg_x2"],
                             in_=Xc(0)), after=[t_x2_chunk[0], t_x2_chunk[-1]], tag="dbg_x2"))
        MXROW = BCf[0:1, 1024:2048]
        t_m3cp = []
        prev_s3 = None
        svar = {}
        for hh in range(2):
            lo, hi = hh * 512, (hh + 1) * 512
            s1r = PS[2][0:1, 0:512]
            s2r = PS[3][0:1, 0:512]
            cps = []
            for j in range(ND):
                cps.append(em.op("vector", lambda e, j=j, lo=lo, hi=hi: e.tensor_copy(
                    BB[:, 8192 + j * 512:8192 + (j + 1) * 512], Xc(j, lo, hi)),
                    after=[t_x2_chunk[j]] + ([svar[0]] if hh else [bcq1, bcq2]) +
                    ([prev_s3] if prev_s3 else []),
                    tag=f"3cp_{hh}_{j}"))
            def s1fn(e, s1r=s1r):
                ins = None
                for j in range(ND):
                    ins = e.matmul(s1r, ONESB[:], BB[:, 8192 + j * 512:8192 + (j + 1) * 512],
                                   start=(j == 0), stop=(j == ND - 1))
                return ins
            tmm = em.op("tensor", s1fn, after=cps + [t_onesb], tag=f"3s1_{hh}")
            sq = []
            for j in range(ND):
                sq.append(em.op("scalar", lambda e, j=j, lo=lo, hi=hi: e.activation(
                    BB[:, j * 512:(j + 1) * 512], Xc(j, lo, hi), AF.Square),
                    after=[t_x2_chunk[j]] + ([prev_s3] if prev_s3 else []),
                    tag=f"sq3_{hh}_{j}"))
            def s2fn(e, s2r=s2r):
                ins = None
                for j in range(ND):
                    ins = e.matmul(s2r, ONESB[:], BB[:, j * 512:(j + 1) * 512],
                                   start=(j == 0), stop=(j == ND - 1))
                return ins
            tmm2 = em.op("tensor", s2fn,
                         after=sq + ([prev_s2b] if hh == 0 else [svar[0]]), tag=f"3s2_{hh}")
            prev_s3 = tmm2
            AR = STAT[32:33, 0:512]
            k1 = em.op("vector", lambda e, lo=lo, hi=hi, s1r=s1r: e.tensor_copy(
                STAT[0:1, lo:hi], s1r),
                after=[tmm, tmm2] + ([bcq1, bcq2] if hh == 0 else []), tag=f"3mc_{hh}")
            if KDBG and hh == 0:
                k1 = em.op("sync", lambda e: e.dma_start(
                    out=dbg["dbg_s1raw"], in_=STAT[0:1, 0:512]), after=[k1], tag="dbg_s1raw")
                t_out_dbg.append(k1)
            k1 = em.op("vector", lambda e, lo=lo, hi=hi: e.tensor_single_scalar(
                STAT[0:1, lo:hi], STAT[0:1, lo:hi], 1.0 / D, ALU.mult),
                after=[k1], tag=f"3m_{hh}")
            if KDBG and hh == 0:
                k1 = em.op("sync", lambda e: e.dma_start(
                    out=dbg["dbg_m3"], in_=STAT[0:1, 0:512]), after=[k1], tag="dbg_m3")
                t_out_dbg.append(k1)
            k2 = em.op("scalar", lambda e, lo=lo, hi=hi, AR=AR: e.activation(
                AR, STAT[0:1, lo:hi], AF.Square),
                after=[k1] + t_glu_last, tag=f"3msq_{hh}")
            k3 = em.op("vector", lambda e, lo=lo, hi=hi, s2r=s2r, AR=AR: e.scalar_tensor_tensor(
                BCROW[0:1, lo:hi], s2r, 1.0 / D, AR, ALU.mult, ALU.subtract),
                after=[k2], tag=f"3var_{hh}")
            k4 = em.op("scalar", lambda e, lo=lo, hi=hi, AR=AR: e.activation(
                AR, BCROW[0:1, lo:hi], AF.Sqrt, bias=V0(C_ELN)),
                after=[k3], tag=f"3sd_{hh}")
            k3 = em.op("vector", lambda e, lo=lo, hi=hi, AR=AR: e.reciprocal(
                BCROW[0:1, lo:hi], AR), after=[k4], tag=f"3r_{hh}")
            if KDBG and hh == 0:
                kcp = em.op("vector", lambda e, s2r=s2r: e.tensor_copy(
                    dbg and SPR[0:1, 0:1] or None, s2r[0:1, 0:1]) if False else
                    e.tensor_copy(MXROW[0:1, 512:1024].bitcast(F32)[0:1, 0:512]
                                  if False else BCROW[0:1, 512:1024], s2r),
                    after=[tmm2, k3], tag="dbg_s2cp") if False else None
                k3dma = em.op("sync", lambda e: e.dma_start(
                    out=dbg["dbg_s2r"], in_=BCROW[0:1, 0:512]), after=[k3], tag="dbg_s2r")
                t_out_dbg.append(k3dma)
            svar[hh] = k3
        bm1 = em.op("tensor", lambda e: e.matmul(PS[4][:], ONESR[:], STAT[0:1, 0:512],
                                                 start=True, stop=True),
                    after=[svar[0], svar[1]] + t_x1n_chunk, tag="bm3a")
        bm2 = em.op("tensor", lambda e: e.matmul(PS[5][:], ONESR[:], STAT[0:1, 512:1024],
                                                 start=True, stop=True),
                    after=[svar[1]], tag="bm3b")
        d4 = svar[1]
        if KDBG:
            d4 = em.op("sync", lambda e: e.dma_start(out=dbg["dbg_var"], in_=BCROW),
                       after=[svar[0], svar[1]], tag="dbg_var")
            t_out_dbg.append(d4)
        d5 = em.op("vector", lambda e: e.tensor_copy(STAT[0:1, 0:1024], BCROW),
                   after=[d4, bm1, bm2, svar[0], svar[1]], tag="ln3r")
        if KDBG:
            d5 = em.op("sync", lambda e: e.dma_start(out=dbg["dbg_rstd3"], in_=STAT[0:1, 0:1024]),
                       after=[d5], tag="dbg_rstd3")
            t_out_dbg.append(d5)
        t_t3 = []
        for j in range(ND):
            for h, bm in ((0, PS[4]), (1, PS[5])):
                t_t3.append(em.op("vector", lambda e, j=j, h=h, bm=bm: e.tensor_tensor(
                    X1c(j, h * 512, (h + 1) * 512), Xc(j, h * 512, (h + 1) * 512), bm[:],
                    ALU.subtract),
                    after=[bm1, bm2, prev_s3], tag=f"t3_{j}_{h}"))
        tab = em.op("scalar", lambda e: e.activation(BCf[:, 0:1024], X1c(0), AF.Abs),
                    after=t_t3[0:2] + [d4, d5], tag="abs0")
        ttree = tab
        for j in range(1, ND):
            ta2 = em.op("scalar", lambda e, j=j: e.activation(
                BCf[:, 1024:2048], X1c(j), AF.Abs),
                after=[t_t3[2 * j], t_t3[2 * j + 1], ttree, (d5 if j == 1 else None)],
                tag=f"abs{j}")
            ttree = em.op("vector", lambda e: e.tensor_tensor(
                BCf[:, 0:1024], BCf[:, 0:1024], BCf[:, 1024:2048], ALU.max),
                after=[ta2], tag=f"tmax{j}")
        trow = []
        prevt = ttree
        for bi in range(8):
            tt_ = em.op("tensor", lambda e, bi=bi: e.transpose(
                PS[6][:, 0:128], BCf[:, bi * 128:(bi + 1) * 128], EYE[:]),
                after=[prevt, t_eye, svar[1]], tag=f"tr{bi}")
            tr_ = em.op("vector", lambda e: e.reduce_max(SPR[:, 76:77], PS[6][:, 0:128], AX.X),
                        after=[tt_], tag=f"trm{bi}")
            tt2 = em.op("tensor", lambda e: e.transpose(PS[7][0:1, 0:128], SPR[:, 76:77],
                                                        EYE[:]),
                        after=[tr_], tag=f"tr2_{bi}")
            prevt = em.op("vector", lambda e, bi=bi: e.tensor_copy(
                MXROW[0:1, bi * 128:(bi + 1) * 128], PS[7][0:1, 0:128]),
                after=[tt2, ttree], tag=f"trcp{bi}")
            trow.append(prevt)
        if KDBG:
            trow = trow + [em.op("sync", lambda e: e.dma_start(
                out=dbg["dbg_absmax"], in_=MXROW), after=trow, tag="dbg_absmax")]
            t_out_dbg.append(trow[-1])
        e1 = em.op("vector", lambda e: e.tensor_tensor(MXROW, MXROW, STAT[0:1, 0:1024],
                                                       ALU.mult), after=trow + [d5], tag="maxz")
        e2_ = em.op("vector", lambda e: e.tensor_single_scalar(MXROW, MXROW, EPS_Q, ALU.max),
                    after=[e1], tag="clipz")
        e3 = em.op("vector", lambda e: e.reciprocal(BCROW, MXROW), after=[e2_], tag="invz")
        e4 = em.op("vector", lambda e: e.tensor_tensor(BCROW, BCROW, STAT[0:1, 0:1024],
                                                       ALU.mult), after=[e3], tag="crs")
        e5 = em.op("vector", lambda e: e.tensor_single_scalar(BCROW, BCROW, 127.0, ALU.mult),
                   after=[e4], tag="crow")
        g1_ = em.op("vector", lambda e: e.tensor_single_scalar(
            STAT[0:1, 0:1024], MXROW, V0(VC_W1SCL), ALU.mult),
            after=[e4, t_w1scl], tag="grow")
        bq1_ = em.op("tensor", lambda e: e.matmul(PS[4][:], ONESR[:], BCROW[0:1, 0:512],
                                                  start=True, stop=True),
                     after=[e5] + t_t3, tag="bqc1")
        bq2_ = em.op("tensor", lambda e: e.matmul(PS[5][:], ONESR[:], BCROW[0:1, 512:1024],
                                                  start=True, stop=True),
                     after=[e5], tag="bqc2")
        if KDBG:
            tdc = em.op("sync", lambda e: e.dma_start(out=dbg["dbg_crow"], in_=BCROW),
                        after=[e5, g1_], tag="dbg_crow")
            tdm_ = em.op("sync", lambda e: e.dma_start(out=dbg["dbg_mxrow"], in_=MXROW),
                         after=[e2_, tdc], tag="dbg_mxrow")
            e5 = tdc
            g1_ = em.op("vector", lambda e: e.engine_nop(), after=[g1_, tdm_], tag="g1b")
            t_out_dbg += [tdc, tdm_]
        t_qz_chunk = []
        prevqz = None
        for j in range(ND):
            tk = None
            for h, bcx in ((0, PS[4]), (1, PS[5])):
                ti_ = em.op("vector", lambda e, j=j, h=h, bcx=bcx: e.tensor_tensor(
                    QZI, X1c(j, h * 512, (h + 1) * 512), bcx[:], ALU.mult),
                    after=[bq1_, bq2_, ttree, g1_] + ([prevqz] if prevqz else []),
                    tag=f"qzi{j}_{h}")
                tk = em.op("vector", lambda e, j=j, h=h: e.tensor_copy(
                    QZc(j, h * 512, (h + 1) * 512), QZI),
                    after=[ti_, t_x1n_chunk[j]], tag=f"qzc{j}_{h}")
                prevqz = tk
            t_qz_chunk.append(tk)
            if KDBG and j == 0:
                t_out_dbg.append(em.op("sync", lambda e, tk=tk: e.dma_start(out=dbg["dbg_qz"],
                                 in_=QZc(0)), after=[tk], tag="dbg_qz"))
        if KDBG:
            t_out_dbg.append(em.op("sync", lambda e: e.dma_start(out=dbg["dbg_g"],
                             in_=STAT[0:1, 0:T]), after=[g1_], tag="dbg_g"))

        q1ring = Ring(4)
        w1fring = Ring(2)
        q2ring = Ring(2)
        w2fring = Ring(2)
        t_w1q = {}
        t_w2q = {}
        mm1_free = {0: None, 1: None}
        prev_q_evacs = t_qz_chunk + [g1_, t_w2scl, t_ws1, t_ws2]
        prev_mm2_last = None
        t_rmax_cons = None
        t_out_dma = []
        last_w1quant = None
        for q in range(NQ):
            qlo = q * QT
            trm = em.op("vector", lambda e: e.memset(RMAX[:], -1e30),
                        after=prev_q_evacs + ([t_rmax_cons] if t_rmax_cons else []),
                        tag=f"rmz{q}")
            rm_tok = trm
            t_relu = {}
            first_m = True
            for m in range(NFF):
                if q == 0:
                    sF, fF = w1fring.slot()
                    tlf = em.op("sync", lambda e, m=m, s=sF: e.dma_start(
                        out=W1F(s).rearrange("p (k m) -> p k m", m=128),
                        in_=w1t_v[:, :, m * 128:(m + 1) * 128]),
                        after=fF + (t_qz_chunk + [ttree, g1_] if m < 2 else []),
                        tag=f"w1f{m}")
                    sQ, fQ = q1ring.slot()
                    tq_prev = None
                    for hh in range(2):
                        tqa = em.op("vector", lambda e, s=sF, hh=hh: e.tensor_scalar(
                            WQ32H, W1F(s)[:, hh * 1024:(hh + 1) * 1024], V(VC_WS1), None,
                            ALU.mult),
                            after=[tlf, t_ws1, last_w1quant] + fQ, tag=f"w1r{m}_{hh}")
                        tq_prev = em.op("vector", lambda e, s=sQ, hh=hh: e.tensor_scalar(
                            QW1S(s)[:, hh * 1024:(hh + 1) * 1024], WQ32H, 1, -1,
                            ALU.min, ALU.max), after=[tqa], tag=f"w1c{m}_{hh}")
                        last_w1quant = tq_prev
                    w1fring.consumed(sF, [last_w1quant])
                    twc = em.op("sync", lambda e, m=m, s=sQ: e.dma_start(
                        out=w1q[m], in_=QW1S(s)), after=[tq_prev], tag=f"w1qw{m}")
                    t_w1q[m] = twc
                    wtok, wslot, wext = tq_prev, sQ, [twc]
                else:
                    sQ, fQ = q1ring.slot()
                    wtok = em.op("sync", lambda e, m=m, s=sQ: e.dma_start(
                        out=QW1S(s), in_=w1q[m]),
                        after=fQ + [t_w1q[m]] + (prev_q_evacs if m < 2 else []),
                        tag=f"w1l{q}_{m}")
                    wslot, wext = sQ, []
                bidx = len(t_relu) % 2
                bfree = mm1_free[bidx]
                def mmfn(e, s=wslot, b=bidx, q=q):
                    ins = None
                    for k in range(ND):
                        ins = e.matmul(PS[b][:, 0:QT], QW1S(s)[:, k * 128:(k + 1) * 128],
                                       QZc(k, q * QT, (q + 1) * QT),
                                       start=(k == 0), stop=(k == ND - 1))
                    return ins
                tmf = em.op("tensor", mmfn,
                            after=[wtok, bfree] + (t_qz_chunk if q == 0 and m < 2 else []) +
                            ([prev_mm2_last] if m < 2 and prev_mm2_last else []),
                            tag=f"m1_{q}_{m}")
                trl = em.op("scalar", lambda e, m=m, b=bidx: e.activation(
                    HQc(m), PS[b][:, 0:QT], AF.Relu),
                    after=[tmf] + ([ttree, d4] if q == 0 and m < 2 else []), tag=f"rl{q}_{m}")
                rm_tok = em.op("vector", lambda e, b=bidx: e.tensor_tensor(
                    RMAX[:], RMAX[:], PS[b][:, 0:QT], ALU.max),
                    after=[trl, rm_tok], tag=f"rmx{q}_{m}")
                mm1_free[bidx] = rm_tok
                q1ring.consumed(wslot, [rm_tok] + wext)
                if KDBG and q == 0 and m == 0:
                    t_out_dbg.append(em.op("sync", lambda e: e.dma_start(out=dbg["dbg_hq"],
                                     in_=HQc(0)), after=[trl], tag="dbg_hq"))
                t_relu[m] = trl
                first_m = False
            prevh = rm_tok
            for bi in range(2):
                tt_ = em.op("tensor", lambda e, bi=bi: e.transpose(
                    PS[2][:, 0:128], RMAX[:, bi * 128:(bi + 1) * 128], EYE[:]),
                    after=[prevh], tag=f"htr{q}_{bi}")
                tr_ = em.op("vector", lambda e: e.reduce_max(SPR[:, 76:77], PS[2][:, 0:128],
                                                             AX.X), after=[tt_], tag=f"htm{q}_{bi}")
                tt2 = em.op("tensor", lambda e: e.transpose(PS[3][0:1, 0:128], SPR[:, 76:77],
                                                            EYE[:]), after=[tr_], tag=f"ht2{q}_{bi}")
                prevh = em.op("vector", lambda e, bi=bi: e.tensor_copy(
                    ROWA[0:1, bi * 128:(bi + 1) * 128], PS[3][0:1, 0:128]),
                    after=[tt2], tag=f"htc{q}_{bi}")
            t_rmax_cons = prevh
            f1_ = em.op("vector", lambda e, qlo=qlo: e.tensor_tensor(
                ROWB[0:1, 0:QT], ROWA[0:1, 0:QT], STAT[0:1, qlo:qlo + QT], ALU.mult),
                after=[prevh, g1_], tag=f"mg{q}")
            f2_ = em.op("vector", lambda e: e.tensor_single_scalar(
                ROWB[0:1, 0:QT], ROWB[0:1, 0:QT], EPS_Q, ALU.max), after=[f1_], tag=f"mgc{q}")
            f3_ = em.op("vector", lambda e: e.tensor_single_scalar(
                ROWA[0:1, 0:QT], ROWB[0:1, 0:QT], V0(VC_W2SCL), ALU.mult),
                after=[f2_, t_w2scl], tag=f"spr{q}")
            f4_ = em.op("vector", lambda e: e.reciprocal(ROWC[0:1, 0:QT], ROWB[0:1, 0:QT]),
                        after=[f2_, f3_], tag=f"ei{q}")
            f5_ = em.op("vector", lambda e, qlo=qlo: e.tensor_tensor(
                ROWC[0:1, 0:QT], ROWC[0:1, 0:QT], STAT[0:1, qlo:qlo + QT], ALU.mult),
                after=[f4_], tag=f"eg{q}")
            f6_ = em.op("vector", lambda e: e.tensor_single_scalar(
                ROWC[0:1, 0:QT], ROWC[0:1, 0:QT], 127.0, ALU.mult), after=[f5_], tag=f"e{q}")
            be_ = em.op("tensor", lambda e: e.matmul(PS[2][:, 0:QT], ONESR[:], ROWC[0:1, 0:QT],
                                                     start=True, stop=True),
                        after=[f6_], tag=f"bce{q}")
            bs_ = em.op("tensor", lambda e: e.matmul(PS[3][:, 0:QT], ONESR[:], ROWA[0:1, 0:QT],
                                                     start=True, stop=True),
                        after=[f3_, f6_], tag=f"bcs{q}")
            tsp = em.op("vector", lambda e: e.tensor_copy(SPRS[:, 0:QT], PS[3][:, 0:QT]),
                        after=[bs_, be_], tag=f"spc{q}")
            qh_i = {}
            t_hq = {}
            for m in range(NFF):
                ii = m % 2
                ti_ = em.op("vector", lambda e, m=m, ii=ii: e.tensor_tensor(
                    QI32[ii], HQc(m), PS[2][:, 0:QT], ALU.mult),
                    after=[be_, t_relu[m]] + ([t_hq[m - 2]] if m >= 2 else []),
                    tag=f"qi{q}_{m}")
                qh_i[m] = ti_
                t_hq[m] = em.op("gpsimd", lambda e, m=m, ii=ii: e.tensor_copy(HQc(m), QI32[ii]),
                                after=[ti_], kind="cmp", tag=f"qc{q}_{m}")
            prev_k = None
            for k in range(NFF):
                if q == 0:
                    sF, fF = w2fring.slot()
                    tlf = em.op("sync", lambda e, k=k, s=sF: e.dma_start(
                        out=W2F(s), in_=w2t[k * 128:(k + 1) * 128, :]),
                        after=fF + ([last_w1quant, t_qz_chunk[-1]] if k < 2 else []),
                        tag=f"w2f{k}")
                    sQ2, fQ2 = q2ring.slot()
                    tq_prev = None
                    for hh in range(2):
                        tqa = em.op("vector", lambda e, s=sF, hh=hh: e.tensor_scalar(
                            WQ32H, W2F(s)[:, hh * 1024:(hh + 1) * 1024], V(VC_WS2), None,
                            ALU.mult),
                            after=[tlf, t_ws2, last_w1quant] + fQ2, tag=f"w2r{k}_{hh}")
                        tq_prev = em.op("vector", lambda e, s=sQ2, hh=hh: e.tensor_scalar(
                            QW2S(s)[:, hh * 1024:(hh + 1) * 1024], WQ32H, 1, -1,
                            ALU.min, ALU.max), after=[tqa], tag=f"w2c{k}_{hh}")
                    w2fring.consumed(sF, [tq_prev])
                    twc = em.op("sync", lambda e, k=k, s=sQ2: e.dma_start(
                        out=w2q[k], in_=QW2S(s)), after=[tq_prev], tag=f"w2qw{k}")
                    t_w2q[k] = twc
                    wtok2, ws2slot, wext2 = tq_prev, sQ2, [twc]
                else:
                    sQ2, fQ2 = q2ring.slot()
                    wtok2 = em.op("sync", lambda e, k=k, s=sQ2: e.dma_start(
                        out=QW2S(s), in_=w2q[k]), after=fQ2 + [t_w2q[k]], tag=f"w2l{q}_{k}")
                    ws2slot, wext2 = sQ2, []
                extra0 = []
                if k == 0:
                    extra0 = [qh_i[NFF - 1], tsp] + [t for t in
                                                     ([prev_mm2_last] if prev_mm2_last else [])]
                    extra0 += [rm_tok]
                def mmfn2(e, s=ws2slot, k=k):
                    ins = None
                    for o in range(16):
                        ins = e.matmul(PS[o // 2][:, (o % 2) * QT:(o % 2 + 1) * QT],
                                       QW2S(s)[:, o * 128:(o + 1) * 128], HQc(k),
                                       start=(k == 0 and o % 2 == 0), stop=(k == NFF - 1),
                                       skip_group_check=True)
                    return ins
                tmk = em.op("tensor", mmfn2,
                            after=[wtok2, t_hq[k]] + extra0 + ([prev_k] if prev_k else []),
                            tag=f"m2_{q}_{k}")
                prev_k = tmk
                q2ring.consumed(ws2slot, [tmk] + wext2)
            prev_mm2_last = prev_k
            evacs = []
            for o in range(16):
                tev1 = em.op("vector", lambda e, o=o: e.tensor_tensor(
                    EVS[:, 0:QT], PS[o // 2][:, (o % 2) * QT:(o % 2 + 1) * QT], SPRS[:, 0:QT],
                    ALU.mult),
                    after=[prev_k, tsp] + ([t_hq[NFF - 1], t_hq[NFF - 2]] if o == 0 else []),
                    tag=f"e1_{q}_{o}")
                tev2 = em.op("vector", lambda e, o=o, q=q: e.tensor_tensor(
                    Xc(o, q * QT, (q + 1) * QT), EVS[:, 0:QT], Xc(o, q * QT, (q + 1) * QT),
                    ALU.add), after=[tev1], tag=f"e2_{q}_{o}")
                evacs.append(tev2)
                t_out_dma.append(em.op("sync", lambda e, o=o, q=q: e.dma_start(
                    out=out_d[o * 128:(o + 1) * 128, q * QT:(q + 1) * QT],
                    in_=Xc(o, q * QT, (q + 1) * QT)), after=[tev2], tag=f"out{q}_{o}"))
            prev_q_evacs = evacs
            mm1_free = {0: evacs[3], 1: evacs[3]}

        em.op("vector", lambda e: e.engine_nop(),
              after=t_out_dma + (t_out_dbg if KDBG else []), tag="final")
        em.check()
        em.emit(block, sems)
    return nc


_NC_CACHE = [None]
_LAST_RES = [None]


def kernel(**inputs):
    x = np.asarray(inputs["L0_attn_x"], dtype=np.float32)
    B, S, _ = x.shape
    f32 = lambda a: np.ascontiguousarray(np.asarray(a), dtype=np.float32)

    def vec16(a):
        return f32(a).reshape(16, 128).T.copy()

    lw = f32(inputs["L0_attn_left_W"]); rw = f32(inputs["L0_attn_right_W"])
    gaw = f32(inputs["L0_attn_ga_W"]); gxw = f32(inputs["L0_attn_gx_W"])
    ow = f32(inputs["L0_attn_out_W"])
    u1w = f32(inputs["L0_attn_up1_W"]); u2w = f32(inputs["L0_attn_up2_W"])
    dwt = f32(inputs["L0_attn_down_W"])
    w1t = np.ascontiguousarray(f32(inputs["L0_ffn_w1"]).T)
    w2t = np.ascontiguousarray(f32(inputs["L0_ffn_w2"]).T)
    eye = np.eye(128, dtype=np.float32)
    upbv = np.concatenate([f32(inputs["L0_attn_up1_b"]).reshape(48, 128).T,
                           f32(inputs["L0_attn_up2_b"]).reshape(48, 128).T], axis=1).copy()
    cwv = f32(inputs["L0_attn_conv_w"]).reshape(4, 16, 128).transpose(2, 0, 1).reshape(128, 64)
    cwv = np.ascontiguousarray(cwv)

    base_vec = np.zeros((128, 192), dtype=np.float32)
    base_vec[:, VC_LB:VC_LB + 16] = vec16(inputs["L0_attn_left_b"])
    base_vec[:, VC_RB:VC_RB + 16] = vec16(inputs["L0_attn_right_b"])
    base_vec[:, VC_CB:VC_CB + 16] = vec16(inputs["L0_attn_conv_b"])
    base_vec[:, VC_GAB:VC_GAB + 16] = vec16(inputs["L0_attn_ga_b"])
    base_vec[:, VC_GXB:VC_GXB + 16] = vec16(inputs["L0_attn_gx_b"])
    base_vec[:, VC_OB:VC_OB + 16] = vec16(inputs["L0_attn_out_b"])
    base_vec[:, VC_DB:VC_DB + 16] = vec16(inputs["L0_attn_down_b"])
    base_vec[:, VC_SC:VC_SC + 16] = vec16(inputs["L0_attn_a_param"])
    base_vec[:, VC_N1W:VC_N1W + 16] = vec16(inputs["L0_attn_norm1_w"])
    base_vec[:, VC_N2W:VC_N2W + 16] = vec16(inputs["L0_attn_norm2_w"])

    in_maps = []
    for c in range(8):
        b, h = c // 2, c % 2
        main = x[b, h * T:(h + 1) * T, :].T
        if h == 1:
            halo = x[b, T - HALO:T, :].T
        else:
            halo = np.zeros((D, HALO), dtype=np.float32)
        xin = np.ascontiguousarray(np.concatenate([halo, main], axis=1))
        vec = base_vec.copy()
        vec[:, VC_MASK] = float(h)
        vec[:, VC_SEL + b] = 1.0 if h == 0 else 0.0
        vec[:, VC_RSEL + b] = 1.0 if h == 1 else 0.0
        in_maps.append({
            "xin": xin, "lw": lw, "rw": rw, "gaw": gaw, "gxw": gxw, "ow": ow,
            "u1w": u1w, "u2w": u2w, "dwt": dwt, "w1t": w1t, "w2t": w2t,
            "w1sl": np.ascontiguousarray(w1t[:, c * 1024:(c + 1) * 1024]),
            "w2sl": np.ascontiguousarray(w2t[:, c * 256:(c + 1) * 256]),
            "vecs": vec, "upbv": upbv, "cwv": cwv, "eyev": eye,
        })

    if _NC_CACHE[0] is None:
        _NC_CACHE[0] = build_nc()
    res = run_bass_kernel_spmd(_NC_CACHE[0], in_maps, core_ids=list(range(8)))
    _LAST_RES[0] = res
    out = np.empty((B, S, D), dtype=np.float32)
    for c in range(8):
        b, h = c // 2, c % 2
        out[b, h * T:(h + 1) * T, :] = np.asarray(res.results[c]["out"]).T
    return out


if __name__ == "__main__":
    build_nc()
    print("build ok; total_sb =", TOTAL_SB)
